# version 01 (final)
# baseline (speedup 1.0000x reference)
import math
import numpy as np
import concourse.bass as bass
import concourse.mybir as mybir
from concourse.bass_utils import run_bass_kernel_spmd

F32 = mybir.dt.float32
BF16 = mybir.dt.bfloat16
AF = mybir.ActivationFunctionType
OP = mybir.AluOpType
AX = mybir.AxisListType

D = 4096
S = 2048
NT = 16
KC = 32
NH = 32
HD = 128
NE = 64
RANK = 8
BN = 512
SCALE = 1.0 / math.sqrt(HD)
EPS = 1e-5
NEG = -1e9
SLOPES = [1.0 / (2.0 ** (0.25 * (h + 1))) for h in range(NH)]


def build_nc():
    nc = bass.Bass("TRN2", target_bir_lowering=False)
    X = nc.dram_tensor("X", [S, D], F32, kind="ExternalInput").ap()
    WQKVT = nc.dram_tensor("WQKVT", [D, 3 * D], F32, kind="ExternalInput").ap()
    WOUTT = nc.dram_tensor("WOUTT", [D, D], F32, kind="ExternalInput").ap()
    AT = nc.dram_tensor("AT", [D, BN], F32, kind="ExternalInput").ap()
    BT = nc.dram_tensor("BT", [BN, D], F32, kind="ExternalInput").ap()
    WQ = nc.dram_tensor("WQ", [16, D], F32, kind="ExternalInput").ap()
    KEYST = nc.dram_tensor("KEYST", [16, NE], F32, kind="ExternalInput").ap()
    QLW = nc.dram_tensor("QLW", [D], F32, kind="ExternalInput").ap()
    QLB = nc.dram_tensor("QLB", [D], F32, kind="ExternalInput").ap()
    KLW = nc.dram_tensor("KLW", [D], F32, kind="ExternalInput").ap()
    KLB = nc.dram_tensor("KLB", [D], F32, kind="ExternalInput").ap()
    OUT = nc.dram_tensor("OUT", [S, D], F32, kind="ExternalOutput").ap()
    X1 = nc.dram_tensor("X1", [S, D], F32, kind="Internal").ap()
    X2 = nc.dram_tensor("X2", [S, D], F32, kind="Internal").ap()
    Y2T = nc.dram_tensor("Y2Td", [D, S], BF16, kind="Internal").ap()
    QN = nc.dram_tensor("QNd", [S, D], BF16, kind="Internal").ap()
    KN = nc.dram_tensor("KNd", [S, D], BF16, kind="Internal").ap()
    VN = nc.dram_tensor("VNd", [S, D], BF16, kind="Internal").ap()
    QHT = nc.dram_tensor("QHTd", [D, S], BF16, kind="Internal").ap()
    KHT = nc.dram_tensor("KHTd", [D, S], BF16, kind="Internal").ap()
    ATT = nc.dram_tensor("ATTd", [D, S], BF16, kind="Internal").ap()

    state = {"tick": 0, "script": []}

    def emit(eng, fn, inc=1, sync=True, wait=None):
        w = (state["tick"] if wait is None else wait) if sync else None
        state["script"].append((eng, fn, inc, w))
        state["tick"] += inc

    def ewait(eng):
        state["script"].append((eng, None, 0, state["tick"]))

    def flush(block, T):
        script = state["script"]
        state["script"] = []
        engs = []
        for e in script:
            if e[0] not in engs:
                engs.append(e[0])
        for engname in engs:
            entries = [e for e in script if e[0] == engname]

            def body(E, entries=entries):
                for (_, fn, inc, wv) in entries:
                    if wv is not None:
                        E.wait_ge(T, wv)
                    if fn is None:
                        continue
                    inst = fn(E)
                    if inc:
                        inst.then_inc(T, inc)
            getattr(block, engname)(body)

    from contextlib import ExitStack
    with ExitStack() as ctx:
        T = ctx.enter_context(nc.semaphore("T"))
        block = ctx.enter_context(nc.Block())
        IDB = ctx.enter_context(nc.sbuf_tensor("identbf", [128, 128], BF16))
        IDF = ctx.enter_context(nc.sbuf_tensor("identf", [128, 128], F32))
        IOTA = ctx.enter_context(nc.sbuf_tensor("iota", [128, S], F32))
        REL0 = ctx.enter_context(nc.sbuf_tensor("rel0", [128, S], F32))
        MDIAG = ctx.enter_context(nc.sbuf_tensor("mdiag", [128, 128], F32))
        PCOL = ctx.enter_context(nc.sbuf_tensor("pcol", [128, 1], F32))
        LNC = ctx.enter_context(nc.sbuf_tensor("lncols", [128, 128], F32))
        MTbf = ctx.enter_context(nc.sbuf_tensor("mtbf", [128, KC * NE], BF16))
        WQs = ctx.enter_context(nc.sbuf_tensor("wq16", [16, D], F32))
        KEYs = ctx.enter_context(nc.sbuf_tensor("keyst", [16, NE], F32))
        STT = ctx.enter_context(nc.sbuf_tensor("stats", [128, 32], F32))
        MX = ctx.enter_context(nc.sbuf_tensor("mx16", [128, 16], F32))
        G64 = ctx.enter_context(nc.sbuf_tensor("g64", [128, 64], F32))
        S64 = ctx.enter_context(nc.sbuf_tensor("s64", [128, 64], F32))
        W64A = ctx.enter_context(nc.sbuf_tensor("w64a", [128, 64], F32))
        W64B = ctx.enter_context(nc.sbuf_tensor("w64b", [128, 64], F32))
        GE64 = ctx.enter_context(nc.sbuf_tensor("ge64", [128, 64], F32))
        E64 = ctx.enter_context(nc.sbuf_tensor("e64", [128, 64], F32))
        G512 = ctx.enter_context(nc.sbuf_tensor("g512", [128, 512], F32))
        PSM0 = ctx.enter_context(nc.psum_tensor("psm0", [128, 512], F32))
        PSM1 = ctx.enter_context(nc.psum_tensor("psm1", [128, 512], F32))
        PSTR = ctx.enter_context(nc.psum_tensor("pstr", [128, 1024], BF16))
        PSAT = ctx.enter_context(nc.psum_tensor("psat", [128, 128], F32))
        PSSC = ctx.enter_context(nc.psum_tensor("pssc", [128, 64], F32))
        PSAX = ctx.enter_context(nc.psum_tensor("psax", [128, 128], F32))
        emit('gpsimd', lambda E: E.memset(IDB[:, :], 0.0))
        emit('gpsimd', lambda E: E.affine_select(out=IDB[:, :], in_=IDB[:, :], compare_op=OP.not_equal, fill=1.0, base=0, pattern=[[-1, 128]], channel_multiplier=1))
        emit('gpsimd', lambda E: E.memset(IDF[:, :], 0.0))
        emit('gpsimd', lambda E: E.affine_select(out=IDF[:, :], in_=IDF[:, :], compare_op=OP.not_equal, fill=1.0, base=0, pattern=[[-1, 128]], channel_multiplier=1))
        emit('gpsimd', lambda E: E.memset(MDIAG[:, :], 0.0))
        emit('gpsimd', lambda E: E.affine_select(out=MDIAG[:, :], in_=MDIAG[:, :], compare_op=OP.is_ge, fill=NEG, base=0, pattern=[[-1, 128]], channel_multiplier=1))
        emit('vector', lambda E: E.memset(IOTA[:, 0:1], 0.0))
        w = 1
        while w < S:
            emit('vector', lambda E, w=w: E.tensor_scalar(IOTA[:, w:2 * w], IOTA[:, 0:w], float(w), None, op0=OP.add))
            w *= 2
        emit('tensor', lambda E: E.transpose(PSAX[:, :], IOTA[:, 0:128], IDF[:, :]))
        emit('vector', lambda E: E.tensor_copy(PCOL[:, :], PSAX[:, 0:1]))
        emit('vector', lambda E: E.tensor_scalar(REL0[:, :], IOTA[:, :], PCOL[:, 0:1], None, op0=OP.subtract))
        for j, src in enumerate([QLW, QLB, KLW, KLB]):
            for kc in range(KC):
                emit('sync', lambda E, j=j, kc=kc, src=src: E.dma_start(out=LNC[:, j * 32 + kc:j * 32 + kc + 1], in_=src[kc * 128:(kc + 1) * 128].rearrange("(p one) -> p one", one=1)), inc=16)
        emit('sync', lambda E: E.dma_start(out=WQs[:, :], in_=WQ), inc=16)
        emit('sync', lambda E: E.dma_start(out=KEYs[:, :], in_=KEYST), inc=16)
        for kc in range(KC):
            emit('tensor', lambda E, kc=kc: E.matmul(PSAX[:, 0:64], WQs[:, kc * 128:(kc + 1) * 128], KEYs[:, :], start=True, stop=True))
            emit('scalar', lambda E, kc=kc: E.activation(MTbf[:, kc * 64:(kc + 1) * 64], PSAX[:, 0:64], AF.Copy))

        def ffn_phase(SRC, DST, emit_y2t, sfx):
            with (
                nc.sbuf_tensor("atbf" + sfx, [128, KC * 512], BF16) as ATbf,
                nc.sbuf_tensor("btbf" + sfx, [128, 4 * D], BF16) as BTbf,
                nc.sbuf_tensor("xt0" + sfx, [128, D], F32) as XT0,
                nc.sbuf_tensor("xt1" + sfx, [128, D], F32) as XT1,
                nc.sbuf_tensor("fpa" + sfx, [128, D], F32) as FPA,
                nc.sbuf_tensor("x1t" + sfx, [128, D], F32) as X1T,
                nc.sbuf_tensor("ybf" + sfx, [128, D], BF16) as YBF,
                nc.sbuf_tensor("y1t" + sfx, [128, D], BF16) as Y1T,
                nc.sbuf_tensor("trt" + sfx, [128, 1024], BF16) as TRT,
                nc.sbuf_tensor("zg" + sfx, [128, 512], BF16) as ZG,
                nc.sbuf_tensor("zgt" + sfx, [128, 512], BF16) as ZGT,
            ):
                for r in range(4):
                    emit('sync', lambda E, r=r: E.dma_start(out=FPA.rearrange("p (c f) -> p c f", f=512), in_=AT[r * 1024:(r + 1) * 1024, :].rearrange("(c p) f -> p c f", p=128)), inc=16)
                    emit('scalar', lambda E, r=r: E.activation(ATbf[:, r * 4096:(r + 1) * 4096], FPA[:, :], AF.Copy))
                for c in range(4):
                    emit('sync', lambda E, c=c: E.dma_start(out=FPA[:, :], in_=BT[c * 128:(c + 1) * 128, :]), inc=16)
                    emit('scalar', lambda E, c=c: E.activation(BTbf[:, c * 4096:(c + 1) * 4096], FPA[:, :], AF.Copy))

                def ln_into(XSRC, YOUT, scol):
                    emit('vector', lambda E: E.reduce_sum(STT[:, scol + 2:scol + 3], XSRC[:, :], axis=AX.X))
                    emit('scalar', lambda E: E.activation(FPA[:, :], XSRC[:, :], AF.Square))
                    emit('vector', lambda E: E.reduce_sum(STT[:, scol + 3:scol + 4], FPA[:, :], axis=AX.X))
                    emit('vector', lambda E: E.tensor_scalar(STT[:, scol:scol + 1], STT[:, scol + 2:scol + 3], 1.0 / D, None, op0=OP.mult))
                    emit('vector', lambda E: E.tensor_scalar(STT[:, scol + 4:scol + 5], STT[:, scol + 3:scol + 4], 1.0 / D, None, op0=OP.mult))
                    emit('vector', lambda E: E.tensor_tensor(STT[:, scol + 5:scol + 6], STT[:, scol:scol + 1], STT[:, scol:scol + 1], op=OP.mult))
                    emit('vector', lambda E: E.tensor_tensor(STT[:, scol + 6:scol + 7], STT[:, scol + 4:scol + 5], STT[:, scol + 5:scol + 6], op=OP.subtract))
                    emit('vector', lambda E: E.tensor_scalar(STT[:, scol + 6:scol + 7], STT[:, scol + 6:scol + 7], EPS, None, op0=OP.add))
                    emit('scalar', lambda E: E.activation(STT[:, scol + 7:scol + 8], STT[:, scol + 6:scol + 7], AF.Sqrt))
                    emit('vector', lambda E: E.reciprocal(STT[:, scol + 1:scol + 2], STT[:, scol + 7:scol + 8]))
                    emit('vector', lambda E: E.tensor_scalar(YOUT[:, :], XSRC[:, :], STT[:, scol:scol + 1], STT[:, scol + 1:scol + 2], op0=OP.subtract, op1=OP.mult))

                for t in range(NT):
                    XT = XT0 if t % 2 == 0 else XT1
                    emit('sync', lambda E, t=t, XT=XT: E.dma_start(out=XT[:, :], in_=SRC[t * 128:(t + 1) * 128, :]), inc=16)
                    ln_into(XT, YBF, 0)
                    for kc in range(KC):
                        emit('tensor', lambda E, kc=kc: E.transpose(PSTR[:, (kc % 8) * 128:(kc % 8 + 1) * 128], YBF[:, kc * 128:(kc + 1) * 128], IDB[:, :]))
                        if kc % 8 == 7:
                            emit('scalar', lambda E, kc=kc: E.activation(Y1T[:, (kc - 7) * 128:(kc + 1) * 128], PSTR[:, :], AF.Copy))
                    for kc in range(KC):
                        emit('tensor', lambda E, kc=kc: E.matmul(PSM0[:, :], Y1T[:, kc * 128:(kc + 1) * 128], ATbf[:, kc * 512:(kc + 1) * 512], start=(kc == 0), stop=(kc == KC - 1)), inc=(1 if kc == KC - 1 else 0), sync=(kc == 0))
                    for kc in range(KC):
                        emit('tensor', lambda E, kc=kc: E.matmul(PSSC[:, :], Y1T[:, kc * 128:(kc + 1) * 128], MTbf[:, kc * 64:(kc + 1) * 64], start=(kc == 0), stop=(kc == KC - 1)), inc=(1 if kc == KC - 1 else 0), sync=(kc == 0))
                    emit('vector', lambda E: E.tensor_copy(S64[:, :], PSSC[:, :]))
                    emit('vector', lambda E: E.tensor_copy(W64A[:, :], S64[:, :]))
                    emit('vector', lambda E: E.max(out=MX[:, 0:8], in_=W64A[:, :]))
                    emit('vector', lambda E: E.match_replace(out=W64B[:, :], in_to_replace=MX[:, 0:8], in_values=W64A[:, :], imm_value=-1e30))
                    emit('vector', lambda E: E.max(out=MX[:, 8:16], in_=W64B[:, :]))
                    emit('vector', lambda E: E.reduce_max(STT[:, 16:17], S64[:, :], axis=AX.X))
                    emit('vector', lambda E: E.tensor_scalar(STT[:, 17:18], STT[:, 16:17], -1.0, None, op0=OP.mult))
                    emit('vector', lambda E: E.tensor_scalar(W64B[:, 0:8], MX[:, 8:16], -1.0, None, op0=OP.mult))
                    emit('vector', lambda E: E.reduce_max(STT[:, 18:19], W64B[:, 0:8], axis=AX.X))
                    emit('vector', lambda E: E.tensor_scalar(STT[:, 19:20], STT[:, 18:19], -1.0, None, op0=OP.mult))
                    emit('vector', lambda E: E.tensor_scalar(GE64[:, :], S64[:, :], STT[:, 19:20], None, op0=OP.is_ge))
                    emit('scalar', lambda E: E.activation(E64[:, :], S64[:, :], AF.Exp, bias=STT[:, 17:18]))
                    emit('vector', lambda E: E.tensor_tensor(G64[:, :], E64[:, :], GE64[:, :], op=OP.mult))
                    emit('vector', lambda E: E.reduce_sum(STT[:, 20:21], G64[:, :], axis=AX.X))
                    emit('vector', lambda E: E.reciprocal(STT[:, 21:22], STT[:, 20:21]))
                    emit('vector', lambda E: E.tensor_scalar(G64[:, :], G64[:, :], STT[:, 21:22], None, op0=OP.mult))
                    for r in range(RANK):
                        emit('vector', lambda E, r=r: E.tensor_copy(G512.rearrange("p (e u) -> p e u", u=RANK)[:, :, r], G64[:, :]))
                    emit('vector', lambda E: E.tensor_tensor(ZG[:, :], PSM0[:, :], G512[:, :], op=OP.mult))
                    for kc in range(4):
                        emit('tensor', lambda E, kc=kc: E.transpose(PSTR[:, kc * 128:(kc + 1) * 128], ZG[:, kc * 128:(kc + 1) * 128], IDB[:, :]))
                    emit('scalar', lambda E: E.activation(ZGT[:, :], PSTR[:, 0:512], AF.Copy))
                    for n in range(8):
                        for kc in range(4):
                            emit('tensor', lambda E, n=n, kc=kc: E.matmul(PSM1[:, :], ZGT[:, kc * 128:(kc + 1) * 128], BTbf[:, kc * 4096 + n * 512:kc * 4096 + (n + 1) * 512], start=(kc == 0), stop=(kc == 3)), inc=(1 if kc == 3 else 0), sync=(kc == 0))
                        emit('vector', lambda E, n=n, XT=XT: E.scalar_tensor_tensor(out=X1T[:, n * 512:(n + 1) * 512], in0=PSM1[:, :], scalar=1.0, in1=XT[:, n * 512:(n + 1) * 512], op0=OP.mult, op1=OP.add))
                    emit('gpsimd', lambda E, t=t: E.dma_start(out=DST[t * 128:(t + 1) * 128, :], in_=X1T[:, :]), inc=16)
                    if emit_y2t:
                        ln_into(X1T, YBF, 8)
                        for kc in range(KC):
                            emit('tensor', lambda E, kc=kc: E.transpose(PSTR[:, (kc % 8) * 128:(kc % 8 + 1) * 128], YBF[:, kc * 128:(kc + 1) * 128], IDB[:, :]))
                            if kc % 8 == 7:
                                emit('scalar', lambda E, kc=kc: E.activation(TRT[:, :], PSTR[:, :], AF.Copy))
                                emit('gpsimd', lambda E, kc=kc, t=t: E.dma_start(out=Y2T[(kc - 7) * 128:(kc + 1) * 128, t * 128:(t + 1) * 128].rearrange("(c p) q -> p c q", p=128), in_=TRT.rearrange("p (c q) -> p c q", q=128)), inc=16)
                flush(block, T)

        ffn_phase(X, X1, True, "_a")

        with (
            nc.sbuf_tensor("arena", [128, 2 * 16384], BF16) as AR,
            nc.sbuf_tensor("wst", [128, 8192], F32) as WST,
            nc.sbuf_tensor("wbf0", [128, 16384], BF16) as WBF0,
            nc.sbuf_tensor("wbf1", [128, 16384], BF16) as WBF1,
            nc.sbuf_tensor("evb", [128, 512], BF16) as EVB,
        ):
            for g in range(2):
                for r in range(4):
                    emit('sync', lambda E, r=r, g=g: E.dma_start(out=AR.rearrange("p (c q) -> p c q", q=1024)[:, r * 8:(r + 1) * 8, :], in_=Y2T[r * 1024:(r + 1) * 1024, g * 1024:(g + 1) * 1024].rearrange("(c p) q -> p c q", p=128)), inc=16)
                for n in range(24):
                    WBF = WBF0 if n % 2 == 0 else WBF1
                    for h2 in range(2):
                        emit('sync', lambda E, n=n, h2=h2: E.dma_start(out=WST.rearrange("p (c f) -> p c f", f=512), in_=WQKVT[h2 * 2048:(h2 + 1) * 2048, n * 512:(n + 1) * 512].rearrange("(c p) f -> p c f", p=128)), inc=16)
                        emit('scalar', lambda E, h2=h2, WBF=WBF: E.activation(WBF[:, h2 * 8192:(h2 + 1) * 8192], WST[:, :], AF.Copy))
                    for t8 in range(8):
                        PSM = PSM0 if t8 % 2 == 0 else PSM1
                        for kc in range(KC):
                            emit('tensor', lambda E, kc=kc, t8=t8, PSM=PSM, WBF=WBF: E.matmul(PSM[:, :], AR[:, kc * 1024 + t8 * 128:kc * 1024 + (t8 + 1) * 128], WBF[:, kc * 512:(kc + 1) * 512], start=(kc == 0), stop=(kc == KC - 1)), inc=(1 if kc == KC - 1 else 0), sync=(kc == 0))
                        emit('vector', lambda E, PSM=PSM: E.tensor_scalar(EVB[:, :], PSM[:, :], -8.0, 8.0, op0=OP.max, op1=OP.min))
                        if n < 8:
                            DQ, col = QN, n * 512
                        elif n < 16:
                            DQ, col = KN, (n - 8) * 512
                        else:
                            DQ, col = VN, (n - 16) * 512
                        emit('gpsimd', lambda E, DQ=DQ, col=col, g=g, t8=t8: E.dma_start(out=DQ[g * 1024 + t8 * 128:g * 1024 + (t8 + 1) * 128, col:col + 512], in_=EVB[:, :]), inc=16)
            flush(block, T)

        with (
            nc.sbuf_tensor("qkbf", [128, D], BF16) as QKB,
            nc.sbuf_tensor("fpa3", [128, D], F32) as FPA3,
            nc.sbuf_tensor("ubf", [128, D], BF16) as UBF,
            nc.sbuf_tensor("trt3", [128, 1024], BF16) as TRT3,
        ):
            for t in range(NT):
                for which, (SRCD, DSTD, jw, jb) in enumerate([(QN, QHT, 0, 1), (KN, KHT, 2, 3)]):
                    emit('sync', lambda E, SRCD=SRCD, t=t: E.dma_start(out=QKB[:, :], in_=SRCD[t * 128:(t + 1) * 128, :]), inc=16)
                    emit('vector', lambda E: E.reduce_sum(STT[:, 2:3], QKB[:, :], axis=AX.X))
                    emit('scalar', lambda E: E.activation(FPA3[:, :], QKB[:, :], AF.Square))
                    emit('vector', lambda E: E.reduce_sum(STT[:, 3:4], FPA3[:, :], axis=AX.X))
                    emit('vector', lambda E: E.tensor_scalar(STT[:, 0:1], STT[:, 2:3], 1.0 / D, None, op0=OP.mult))
                    emit('vector', lambda E: E.tensor_scalar(STT[:, 4:5], STT[:, 3:4], 1.0 / D, None, op0=OP.mult))
                    emit('vector', lambda E: E.tensor_tensor(STT[:, 5:6], STT[:, 0:1], STT[:, 0:1], op=OP.mult))
                    emit('vector', lambda E: E.tensor_tensor(STT[:, 6:7], STT[:, 4:5], STT[:, 5:6], op=OP.subtract))
                    emit('vector', lambda E: E.tensor_scalar(STT[:, 6:7], STT[:, 6:7], EPS, None, op0=OP.add))
                    emit('scalar', lambda E: E.activation(STT[:, 7:8], STT[:, 6:7], AF.Sqrt))
                    emit('vector', lambda E: E.reciprocal(STT[:, 1:2], STT[:, 7:8]))
                    emit('vector', lambda E: E.tensor_scalar(UBF[:, :], QKB[:, :], STT[:, 0:1], STT[:, 1:2], op0=OP.subtract, op1=OP.mult))
                    for kc in range(KC):
                        emit('tensor', lambda E, kc=kc: E.transpose(PSTR[:, (kc % 8) * 128:(kc % 8 + 1) * 128], UBF[:, kc * 128:(kc + 1) * 128], IDB[:, :]))
                        emit('vector', lambda E, kc=kc, jw=jw, jb=jb: E.tensor_scalar(TRT3[:, (kc % 8) * 128:(kc % 8 + 1) * 128], PSTR[:, (kc % 8) * 128:(kc % 8 + 1) * 128], LNC[:, jw * 32 + kc:jw * 32 + kc + 1], LNC[:, jb * 32 + kc:jb * 32 + kc + 1], op0=OP.mult, op1=OP.add))
                        if kc % 8 == 7:
                            emit('gpsimd', lambda E, kc=kc, t=t, DSTD=DSTD: E.dma_start(out=DSTD[(kc - 7) * 128:(kc + 1) * 128, t * 128:(t + 1) * 128].rearrange("(c p) q -> p c q", p=128), in_=TRT3.rearrange("p (c q) -> p c q", q=128)), inc=16)
            flush(block, T)

        with (
            nc.sbuf_tensor("qth", [128, S], BF16) as QTh,
            nc.sbuf_tensor("kth", [128, S], BF16) as KTh,
            nc.sbuf_tensor("vh", [128, S], BF16) as VH,
            nc.sbuf_tensor("pts", [128, S], BF16) as PTS,
            nc.sbuf_tensor("ss", [128, S], F32) as SS,
            nc.sbuf_tensor("p32", [128, S], F32) as P32,
            nc.sbuf_tensor("sreli", [128, S], F32) as SRELI,
            nc.sbuf_tensor("pbf", [128, S], BF16) as PBF,
            nc.sbuf_tensor("ato", [128, 128], BF16) as ATO,
        ):
            for h in range(NH):
                emit('sync', lambda E, h=h: E.dma_start(out=QTh[:, :], in_=QHT[h * 128:(h + 1) * 128, :]), inc=16)
                emit('sync', lambda E, h=h: E.dma_start(out=KTh[:, :], in_=KHT[h * 128:(h + 1) * 128, :]), inc=16)
                for r in range(2):
                    emit('sync', lambda E, h=h, r=r: E.dma_start(out=VH.rearrange("p (j f) -> p j f", f=128)[:, r * 8:(r + 1) * 8, :], in_=VN[r * 1024:(r + 1) * 1024, h * 128:(h + 1) * 128].rearrange("(j p) f -> p j f", p=128)), inc=16)
                slope = SLOPES[h]
                for i in range(NT):
                    kmax = 128 * (i + 1)
                    emit('vector', lambda E, i=i, slope=slope, kmax=kmax: E.tensor_scalar(SRELI[:, 0:kmax], REL0[:, 0:kmax], float(128 * i), slope, op0=OP.subtract, op1=OP.mult))
                    nj = (kmax + 511) // 512
                    for jc in range(nj):
                        wdt = min(512, kmax - jc * 512)
                        PSM = PSM0 if jc % 2 == 0 else PSM1
                        emit('tensor', lambda E, i=i, jc=jc, wdt=wdt, PSM=PSM: E.matmul(PSM[:, 0:wdt], QTh[:, i * 128:(i + 1) * 128], KTh[:, jc * 512:jc * 512 + wdt], start=True, stop=True))
                        emit('vector', lambda E, jc=jc, wdt=wdt, PSM=PSM: E.scalar_tensor_tensor(out=SS[:, jc * 512:jc * 512 + wdt], in0=PSM[:, 0:wdt], scalar=SCALE, in1=SRELI[:, jc * 512:jc * 512 + wdt], op0=OP.mult, op1=OP.add))
                    emit('vector', lambda E, i=i: E.tensor_tensor(SS[:, i * 128:(i + 1) * 128], SS[:, i * 128:(i + 1) * 128], MDIAG[:, :], op=OP.add))
                    emit('vector', lambda E, kmax=kmax: E.reduce_max(STT[:, 24:25], SS[:, 0:kmax], axis=AX.X))
                    emit('vector', lambda E: E.tensor_scalar(STT[:, 25:26], STT[:, 24:25], -1.0, None, op0=OP.mult))
                    emit('scalar', lambda E, kmax=kmax: E.activation(P32[:, 0:kmax], SS[:, 0:kmax], AF.Exp, bias=STT[:, 25:26]))
                    emit('vector', lambda E, kmax=kmax: E.reduce_sum(STT[:, 26:27], P32[:, 0:kmax], axis=AX.X))
                    emit('vector', lambda E: E.reciprocal(STT[:, 27:28], STT[:, 26:27]))
                    emit('vector', lambda E, kmax=kmax: E.tensor_scalar(PBF[:, 0:kmax], P32[:, 0:kmax], STT[:, 27:28], None, op0=OP.mult))
                    for j in range(i + 1):
                        emit('tensor', lambda E, j=j: E.transpose(PSTR[:, (j % 8) * 128:(j % 8 + 1) * 128], PBF[:, j * 128:(j + 1) * 128], IDB[:, :]))
                        if j % 8 == 7 or j == i:
                            j0 = (j // 8) * 8
                            emit('scalar', lambda E, j0=j0, j=j: E.activation(PTS[:, j0 * 128:(j + 1) * 128], PSTR[:, 0:(j - j0 + 1) * 128], AF.Copy))
                    for j in range(i + 1):
                        emit('tensor', lambda E, j=j, i=i: E.matmul(PSAT[:, :], VH[:, j * 128:(j + 1) * 128], PTS[:, j * 128:(j + 1) * 128], start=(j == 0), stop=(j == i)), inc=(1 if j == i else 0), sync=(j == 0))
                    emit('scalar', lambda E: E.activation(ATO[:, :], PSAT[:, :], AF.Copy))
                    emit('gpsimd', lambda E, h=h, i=i: E.dma_start(out=ATT[h * 128:(h + 1) * 128, i * 128:(i + 1) * 128], in_=ATO[:, :]), inc=16)
            flush(block, T)

        with (
            nc.sbuf_tensor("arena5", [128, 2 * 16384], BF16) as AR5,
            nc.sbuf_tensor("wst5", [128, 8192], F32) as WST5,
            nc.sbuf_tensor("wbf50", [128, 16384], BF16) as WBF50,
            nc.sbuf_tensor("wbf51", [128, 16384], BF16) as WBF51,
            nc.sbuf_tensor("xr", [128, 512], F32) as XR,
            nc.sbuf_tensor("x2b", [128, 512], F32) as X2B,
        ):
            for g in range(2):
                for r in range(4):
                    emit('sync', lambda E, r=r, g=g: E.dma_start(out=AR5.rearrange("p (c q) -> p c q", q=1024)[:, r * 8:(r + 1) * 8, :], in_=ATT[r * 1024:(r + 1) * 1024, g * 1024:(g + 1) * 1024].rearrange("(c p) q -> p c q", p=128)), inc=16)
                for n in range(8):
                    WBF = WBF50 if n % 2 == 0 else WBF51
                    for h2 in range(2):
                        emit('sync', lambda E, n=n, h2=h2: E.dma_start(out=WST5.rearrange("p (c f) -> p c f", f=512), in_=WOUTT[h2 * 2048:(h2 + 1) * 2048, n * 512:(n + 1) * 512].rearrange("(c p) f -> p c f", p=128)), inc=16)
                        emit('scalar', lambda E, h2=h2, WBF=WBF: E.activation(WBF[:, h2 * 8192:(h2 + 1) * 8192], WST5[:, :], AF.Copy))
                    for t8 in range(8):
                        PSM = PSM0 if t8 % 2 == 0 else PSM1
                        for kc in range(KC):
                            emit('tensor', lambda E, kc=kc, t8=t8, PSM=PSM, WBF=WBF: E.matmul(PSM[:, :], AR5[:, kc * 1024 + t8 * 128:kc * 1024 + (t8 + 1) * 128], WBF[:, kc * 512:(kc + 1) * 512], start=(kc == 0), stop=(kc == KC - 1)), inc=(1 if kc == KC - 1 else 0), sync=(kc == 0))
                        emit('sync', lambda E, g=g, t8=t8, n=n: E.dma_start(out=XR[:, :], in_=X1[g * 1024 + t8 * 128:g * 1024 + (t8 + 1) * 128, n * 512:(n + 1) * 512]), inc=16)
                        emit('vector', lambda E, PSM=PSM: E.scalar_tensor_tensor(out=X2B[:, :], in0=PSM[:, :], scalar=1.0, in1=XR[:, :], op0=OP.mult, op1=OP.add))
                        emit('gpsimd', lambda E, g=g, t8=t8, n=n: E.dma_start(out=X2[g * 1024 + t8 * 128:g * 1024 + (t8 + 1) * 128, n * 512:(n + 1) * 512], in_=X2B[:, :]), inc=16)
            flush(block, T)

        ffn_phase(X2, OUT, False, "_b")

        for e in ['tensor', 'vector', 'scalar', 'gpsimd', 'sync']:
            ewait(e)
        flush(block, T)
    return nc


def kernel(L0_attn_hidden_states, L0_attn_w_qkv, L0_attn_q_ln_w, L0_attn_q_ln_b,
           L0_attn_k_ln_w, L0_attn_k_ln_b, L0_attn_w_out, L0_ffn_A_w, L0_ffn_B_w,
           L0_ffn_Wq_w, L0_ffn_keys):
    hs = np.ascontiguousarray(L0_attn_hidden_states)
    shared = {
        "WQKVT": np.ascontiguousarray(np.asarray(L0_attn_w_qkv).T),
        "WOUTT": np.ascontiguousarray(np.asarray(L0_attn_w_out).T),
        "AT": np.ascontiguousarray(np.asarray(L0_ffn_A_w).T),
        "BT": np.ascontiguousarray(np.asarray(L0_ffn_B_w).T),
        "WQ": np.ascontiguousarray(np.asarray(L0_ffn_Wq_w)),
        "KEYST": np.ascontiguousarray(np.asarray(L0_ffn_keys).T),
        "QLW": np.ascontiguousarray(np.asarray(L0_attn_q_ln_w)),
        "QLB": np.ascontiguousarray(np.asarray(L0_attn_q_ln_b)),
        "KLW": np.ascontiguousarray(np.asarray(L0_attn_k_ln_w)),
        "KLB": np.ascontiguousarray(np.asarray(L0_attn_k_ln_b)),
    }
    nc = build_nc()
    in_maps = []
    for b in range(8):
        m = dict(shared)
        m["X"] = np.ascontiguousarray(hs[b])
        in_maps.append(m)
    res = run_bass_kernel_spmd(nc, in_maps, core_ids=list(range(8)))
    out = np.stack([np.asarray(res.results[b]["OUT"]) for b in range(8)], axis=0)
    return out.astype(np.float32)
